# version 01
# baseline (speedup 1.0000x reference)
import numpy as np
import concourse.bass as bass
import concourse.mybir as mybir
from concourse.bass_utils import run_bass_kernel_spmd

F32 = mybir.dt.float32
BF16 = mybir.dt.bfloat16
AF = mybir.ActivationFunctionType
ALU = mybir.AluOpType
AX = mybir.AxisListType
EPS = 1e-5
NCORES = 8
SEQ = 4096
D = 1024
NDC = 8

ENGS = ("sync", "gpsimd", "tensor", "vector", "scalar")


class Ctr:
    def __init__(self, nc, name):
        self.h = nc.alloc_semaphore(name)
        self.n = 0


class Prog:
    def __init__(self, nc):
        self.nc = nc
        self.q = {e: [] for e in ENGS}
        self.last_wait = {}

    def wait(self, eng, ctr, val):
        if val is None or val <= 0:
            return
        key = (eng, id(ctr))
        if self.last_wait.get(key, 0) >= val:
            return
        self.last_wait[key] = val
        self.q[eng].append(lambda E, h=ctr.h, v=val: E.wait_ge(h, v))

    def emit(self, eng, fn, waits=(), inc=None, k=1):
        for w in waits:
            if w is None:
                continue
            c, v = w
            self.wait(eng, c, v)
        if inc is None:
            self.q[eng].append(lambda E, f=fn: f(E))
            return None
        inc.n += k
        v = inc.n
        self.q[eng].append(lambda E, f=fn, h=inc.h, kk=k: f(E).then_inc(h, kk))
        return v

    def run_block(self, block):
        for e in ENGS:
            fns = self.q[e]
            if not fns:
                continue
            dec = getattr(block, e)
            def make(fns):
                def _(E):
                    for f in fns:
                        f(E)
                return _
            dec(make(fns))
        self.q = {e: [] for e in ENGS}


def get_buf(nc, env, name, shape, dtype):
    key = ("buf", name)
    if key not in env:
        env[key] = nc.alloc_sbuf_tensor(name, list(shape), dtype)
    return env[key]


def common_init(nc, pr, env):
    init = Ctr(nc, "c_init")
    env["init"] = init
    identbf = get_buf(nc, env, "identbf", (128, 128), BF16)
    rowidx = get_buf(nc, env, "rowidx", (128, 1), F32)
    colidx = get_buf(nc, env, "colidx", (128, 128), F32)
    epsc = get_buf(nc, env, "epsc", (128, 1), F32)
    zeroc = get_buf(nc, env, "zeroc", (128, 1), F32)
    ones_col = get_buf(nc, env, "ones_col", (128, 1), BF16)
    ones_row = get_buf(nc, env, "ones_row", (1, 128), BF16)
    eind = get_buf(nc, env, "eind", (8, 1024), BF16)
    identf = get_buf(nc, env, "identf", (128, 128), F32)
    rowidx8 = get_buf(nc, env, "rowidx8", (8, 1), F32)
    colidx8 = get_buf(nc, env, "colidx8", (8, 1024), F32)

    ops1 = [
        lambda E: E.iota(rowidx[:, :], [[0, 1]], channel_multiplier=1,
                         allow_small_or_imprecise_dtypes=True),
        lambda E: E.iota(colidx[:, :], [[1, 128]], channel_multiplier=0,
                         allow_small_or_imprecise_dtypes=True),
        lambda E: E.memset(epsc[:, :], EPS),
        lambda E: E.memset(zeroc[:, :], 0.0),
        lambda E: E.memset(ones_col[:, :], 1.0),
        lambda E: E.memset(ones_row[:, :], 1.0),
        lambda E: E.iota(rowidx8[:, :], [[0, 1]], channel_multiplier=1,
                         allow_small_or_imprecise_dtypes=True),
        lambda E: E.iota(colidx8[:, :], [[1, 8], [0, 128]], channel_multiplier=0,
                         allow_small_or_imprecise_dtypes=True),
    ]
    v1 = 0
    for op in ops1:
        v1 = pr.emit("gpsimd", op, inc=init)
    pr.emit("gpsimd", lambda E: E.tensor_scalar(
        identbf[:, :], colidx[:, :], rowidx[:, :], None, ALU.is_equal),
        waits=[(init, v1)], inc=init)
    pr.emit("gpsimd", lambda E: E.tensor_scalar(
        identf[:, :], colidx[:, :], rowidx[:, :], None, ALU.is_equal), inc=init)
    pr.emit("gpsimd", lambda E: E.tensor_scalar(
        eind[:, :], colidx8[:, :], rowidx8[:, :], None, ALU.is_equal), inc=init)



def build_l1ffn(nc, pr, env, xin, xout, W, tag, npass=None, ne=None):
    T = 1024
    MP = T // 128
    NPASS = (SEQ // T) if npass is None else npass
    NTS = T // 512
    NE = (8 if ne is None else ne)

    sX = Ctr(nc, tag + "_sX")
    sRW = Ctr(nc, tag + "_sRW")
    sUG = [Ctr(nc, f"{tag}_sUG{b}") for b in range(4)]
    sWD = [Ctr(nc, f"{tag}_sWD{b}") for b in range(2)]
    sP = Ctr(nc, tag + "_sP")
    sV = Ctr(nc, tag + "_sV")
    sA = Ctr(nc, tag + "_sA")

    acc = get_buf(nc, env, tag + "_acc", (128, T * 8), F32)
    xhat = get_buf(nc, env, tag + "_xhat", (128, NDC * T), BF16)
    xe = get_buf(nc, env, tag + "_xe", (128, NDC * T), BF16)
    hid = get_buf(nc, env, tag + "_hid", (128, 8 * T), BF16)
    wu = [get_buf(nc, env, f"{tag}_wu{b}", (128, NDC * 512), BF16) for b in range(4)]
    wg = [get_buf(nc, env, f"{tag}_wg{b}", (128, NDC * 512), BF16) for b in range(4)]
    wd = [get_buf(nc, env, f"{tag}_wd{b}", (128, 8 * 1024), BF16) for b in range(2)]
    rw_sb = get_buf(nc, env, tag + "_rw", (128, 64), BF16)
    ceT = get_buf(nc, env, tag + "_ceT", (8, T), BF16)
    ceb = get_buf(nc, env, tag + "_ceb", (128, T), BF16)
    sg = [get_buf(nc, env, f"{tag}_sg{b}", (128, 512), BF16) for b in range(2)]
    xh = get_buf(nc, env, tag + "_xh", (128, 1024), BF16)
    st = get_buf(nc, env, tag + "_st", (128, 16), F32)
    r8 = get_buf(nc, env, tag + "_r8", (128, 12), F32)
    ce8 = get_buf(nc, env, tag + "_ce8", (128, 8), BF16)

    ps_a = nc.alloc_psum_tensor(tag + "_psa", [128, 2048], F32)
    ps_d = nc.alloc_psum_tensor(tag + "_psd", [128, 2048], F32)
    slotA = [ps_a[:, i * 512:(i + 1) * 512] for i in range(4)]
    slotD = [ps_d[:, i * 1024:(i + 1) * 1024] for i in range(2)]
    ps_tr = [ps_d[:, 0:64].bitcast(BF16), ps_d[:, 512:576].bitcast(BF16)]
    ps_lg = ps_d[:, 1024:1032]
    ps_ceT = ps_d[0:8, 1536:1600].bitcast(BF16)

    identbf = env[("buf", "identbf")]
    epsc = env[("buf", "epsc")]
    zeroc = env[("buf", "zeroc")]
    eind = env[("buf", "eind")]
    init = env["init"]
    init_w = (init, init.n)

    rw_in = W["rw"].rearrange("(c p) e -> p c e", p=128)
    rw_out = rw_sb[:, :].rearrange("p (c e) -> p c e", e=8)
    rw_done = pr.emit("gpsimd", lambda E: E.dma_start(rw_out, rw_in), inc=sRW, k=16)

    slotA_free = [[] for _ in range(4)]
    slotD_free = [[] for _ in range(2)]
    nextA = [0]

    def take_slotA():
        i = nextA[0] % 4
        nextA[0] += 1
        return i

    wu_lastuse = [0, 0, 0, 0]
    wd_lastuse = [0, 0]
    sg_free = [0, 0]
    xe_lastuse = [0]
    hid_guard = [0]

    def load_upgate(e, q, half, buf):
        c0 = q * 1024 + half * 512
        if e < 7:
            apu = W["up"][e, :, c0:c0 + 512].rearrange("(c p) h -> p c h", p=128)
            apg = W["gate"][e, :, c0:c0 + 512].rearrange("(c p) h -> p c h", p=128)
        else:
            apu = W["sup"][:, c0:c0 + 512].rearrange("(c p) h -> p c h", p=128)
            apg = W["sgate"][:, c0:c0 + 512].rearrange("(c p) h -> p c h", p=128)
        wub = wu[buf][:, :].rearrange("p (c h) -> p c h", h=512)
        wgb = wg[buf][:, :].rearrange("p (c h) -> p c h", h=512)
        pr.emit("gpsimd", lambda E: E.dma_start(wub, apu),
                waits=[(sP, wu_lastuse[buf])], inc=sUG[buf], k=16)
        return pr.emit("gpsimd", lambda E: E.dma_start(wgb, apg), inc=sUG[buf], k=16)

    def load_down(e, q, buf):
        if e < 7:
            ap = W["down"][e, q * 1024:(q + 1) * 1024, :].rearrange(
                "(c p) d -> p c d", p=128)
        else:
            ap = W["sdown"][q * 1024:(q + 1) * 1024, :].rearrange(
                "(c p) d -> p c d", p=128)
        wdb = wd[buf][:, :].rearrange("p (c d) -> p c d", d=1024)
        return pr.emit("gpsimd", lambda E: E.dma_start(wdb, ap),
                       waits=[(sP, wd_lastuse[buf])], inc=sWD[buf], k=16)

    store_done = [0]
    for p in range(NPASS):
        r0 = p * T
        ain = xin[r0:r0 + T, :].rearrange("(m p) d -> p m d", p=128)
        aview = acc[:, :].rearrange("p (m d) -> p m d", d=1024)
        ldx = pr.emit("sync", lambda E, ain=ain, aview=aview: E.dma_start(aview, ain),
                      waits=[(sX, store_done[0])], inc=sX, k=16)

        xhat_all = 0
        ceT_all = 0
        tr_free = [0, 0]
        tr_war = [0]
        lg_free = []
        ceT_ps_free = 0
        ce8_war = [0]
        for m in range(MP):
            src = acc[:, m * 1024:(m + 1) * 1024]
            vs1 = pr.emit("vector", lambda E, src=src: E.bn_stats(st[:, 0:6], src[:, 0:512]),
                          waits=[(sX, ldx)], inc=sV)
            vs2 = pr.emit("vector", lambda E, src=src: E.bn_stats(st[:, 6:12], src[:, 512:1024]),
                          inc=sV)
            va = pr.emit("vector", lambda E: E.bn_aggr(st[:, 12:14], st[:, 0:12]),
                         waits=[(sV, vs2)], inc=sV)
            aa = pr.emit("scalar", lambda E: E.activation(
                st[:, 15:16], st[:, 13:14], AF.Ln, bias=epsc[:, :], scale=1.0),
                waits=[(sV, va), init_w], inc=sA)
            ab = pr.emit("scalar", lambda E: E.activation(
                st[:, 14:15], st[:, 15:16], AF.Exp, bias=zeroc[:, :], scale=-0.5),
                waits=[(sA, aa)], inc=sA)
            vx = pr.emit("vector", lambda E, src=src: E.tensor_scalar(
                xh[:, :], src, st[:, 12:13], st[:, 14:15], ALU.subtract, ALU.mult),
                waits=[(sA, ab), (sP, tr_war[0])], inc=sV)
            for c in range(NDC):
                b = c % 2
                tv = pr.emit("tensor", lambda E, c=c, b=b: E.transpose(
                    ps_tr[b], xh[:, c * 128:(c + 1) * 128], identbf[:, :]),
                    waits=[(sV, vx), (sV, tr_free[b]), init_w], inc=sP)
                cv = pr.emit("vector", lambda E, c=c, b=b, m=m: E.tensor_copy(
                    xhat[:, c * T + m * 128: c * T + (m + 1) * 128], ps_tr[b]),
                    waits=[(sP, tv)], inc=sV)
                tr_free[b] = cv
                xhat_all = cv
            tr_war[0] = tv
            lgv = None
            for c in range(NDC):
                lgv = pr.emit("tensor", lambda E, c=c, m=m: E.matmul(
                    ps_lg, xhat[:, c * T + m * 128: c * T + (m + 1) * 128],
                    rw_sb[:, c * 8:(c + 1) * 8],
                    start=(c == 0), stop=(c == NDC - 1)),
                    waits=([(sV, xhat_all), (sRW, rw_done)] + lg_free) if c == 0 else [],
                    inc=(sP if c == NDC - 1 else None))
            nm = pr.emit("vector", lambda E: E.reduce_max(
                r8[:, 8:9], ps_lg, axis=AX.X, negate=True), waits=[(sP, lgv)], inc=sV)
            ea = pr.emit("scalar", lambda E: E.activation(
                r8[:, 0:8], ps_lg, AF.Exp, bias=r8[:, 8:9], scale=1.0,
                accum_out=r8[:, 9:10]), waits=[(sV, nm)], inc=sA)
            rv = pr.emit("vector", lambda E: E.reciprocal(r8[:, 10:11], r8[:, 9:10]),
                         waits=[(sA, ea)], inc=sV)
            eq = pr.emit("vector", lambda E: E.tensor_scalar(
                ce8[:, :], ps_lg, r8[:, 8:9], 0.0, ALU.add, ALU.is_ge),
                waits=[(sP, ce8_war[0])], inc=sV)
            cv8 = pr.emit("vector", lambda E: E.tensor_scalar(
                ce8[:, :], ce8[:, :], r8[:, 10:11], None, ALU.mult),
                waits=[(sV, rv), (sV, eq)], inc=sV)
            lg_free = [(sV, cv8), (sA, ea)]
            ct = pr.emit("tensor", lambda E: E.transpose(
                ps_ceT, ce8[:, :], identbf[:, :]),
                waits=[(sV, cv8), (sV, ceT_ps_free)], inc=sP)
            ce8_war[0] = ct
            ctv = pr.emit("vector", lambda E, m=m: E.tensor_copy(
                ceT[0:8, m * 128:(m + 1) * 128], ps_ceT), waits=[(sP, ct)], inc=sV)
            ceT_ps_free = ctv
            ceT_all = ctv

        acc_last = 0
        for e in range(NE):
            if e < 7:
                cb_last = 0
                for i2 in range(NTS):
                    si = take_slotA()
                    mv = pr.emit("tensor", lambda E, e=e, i2=i2, si=si: E.matmul(
                        slotA[si], eind[0:8, e * 128:(e + 1) * 128],
                        ceT[0:8, i2 * 512:(i2 + 1) * 512], start=True, stop=True),
                        waits=[(sV, ceT_all), init_w] + slotA_free[si], inc=sP)
                    cbv = pr.emit("vector", lambda E, i2=i2, si=si: E.tensor_copy(
                        ceb[:, i2 * 512:(i2 + 1) * 512], slotA[si]),
                        waits=[(sP, mv)], inc=sV)
                    slotA_free[si] = [(sV, cbv)]
                    cb_last = cbv
                for c in range(NDC):
                    xev = pr.emit("vector", lambda E, c=c: E.tensor_tensor(
                        xe[:, c * T:(c + 1) * T], xhat[:, c * T:(c + 1) * T],
                        ceb[:, :], ALU.mult),
                        waits=[(sV, cb_last), (sV, xhat_all), (sP, xe_lastuse[0])],
                        inc=sV)
            else:
                for c in range(NDC):
                    xev = pr.emit("vector", lambda E, c=c: E.tensor_scalar(
                        xe[:, c * T:(c + 1) * T], xhat[:, c * T:(c + 1) * T],
                        0.5, None, ALU.mult),
                        waits=[(sV, xhat_all), (sP, xe_lastuse[0])], inc=sV)

            for q in range(4):
                upg_bufs = [(2 * (e * 4 + q) + half) % 4 for half in range(2)]
                upg_ready = [(sUG[upg_bufs[half]],
                              load_upgate(e, q, half, upg_bufs[half]))
                             for half in range(2)]
                bD = (e * 4 + q) % 2
                d_ready = (sWD[bD], load_down(e, q, bD))

                hid_last = 0
                for hc in range(8):
                    half = hc // 4
                    hsub = hc % 4
                    b = (2 * (e * 4 + q) + half) % 4
                    for ts in range(NTS):
                        siu = take_slotA()
                        uv = None
                        for dc in range(NDC):
                            uv = pr.emit("tensor", lambda E, b=b, hsub=hsub, dc=dc, ts=ts, siu=siu: E.matmul(
                                slotA[siu],
                                wu[b][:, dc * 512 + hsub * 128: dc * 512 + (hsub + 1) * 128],
                                xe[:, dc * T + ts * 512: dc * T + (ts + 1) * 512],
                                start=(dc == 0), stop=(dc == NDC - 1)),
                                waits=([upg_ready[half], (sV, xev)]
                                       + slotA_free[siu]) if dc == 0 else [],
                                inc=(sP if dc == NDC - 1 else None))
                        wu_lastuse[b] = uv
                        xe_lastuse[0] = uv
                        sig = take_slotA()
                        gv = None
                        for dc in range(NDC):
                            gv = pr.emit("tensor", lambda E, b=b, hsub=hsub, dc=dc, ts=ts, sig=sig: E.matmul(
                                slotA[sig],
                                wg[b][:, dc * 512 + hsub * 128: dc * 512 + (hsub + 1) * 128],
                                xhat[:, dc * T + ts * 512: dc * T + (ts + 1) * 512],
                                start=(dc == 0), stop=(dc == NDC - 1)),
                                waits=slotA_free[sig] if dc == 0 else [],
                                inc=(sP if dc == NDC - 1 else None))
                        wu_lastuse[b] = gv
                        sb = (hc * NTS + ts) % 2
                        av = pr.emit("scalar", lambda E, sb=sb, sig=sig: E.activation(
                            sg[sb][:, :], slotA[sig], AF.Silu, bias=zeroc[:, :]),
                            waits=[(sP, gv), (sV, sg_free[sb])], inc=sA)
                        hv = pr.emit("vector", lambda E, hc=hc, ts=ts, sb=sb, siu=siu: E.tensor_tensor(
                            hid[:, hc * T + ts * 512: hc * T + (ts + 1) * 512],
                            slotA[siu], sg[sb][:, :], ALU.mult),
                            waits=[(sP, uv), (sA, av), (sP, hid_guard[0])], inc=sV)
                        slotA_free[siu] = [(sV, hv)]
                        slotA_free[sig] = [(sV, hv)]
                        sg_free[sb] = hv
                        hid_last = hv
                dv_last = 0
                for m in range(MP):
                    sd = (((e * 4 + q) * MP) + m) % 2
                    dv = None
                    for nh in range(2):
                        for hc in range(8):
                            first = (hc == 0)
                            last = (hc == 7)
                            dv = pr.emit("tensor", lambda E, bD=bD, hc=hc, nh=nh, m=m, sd=sd, first=first, last=last: E.matmul(
                                slotD[sd][:, nh * 512:(nh + 1) * 512],
                                hid[:, hc * T + m * 128: hc * T + (m + 1) * 128],
                                wd[bD][:, hc * 1024 + nh * 512: hc * 1024 + (nh + 1) * 512],
                                start=first, stop=last),
                                waits=([d_ready, (sV, hid_last)]
                                       + (slotD_free[sd] if nh == 0 else []))
                                      if first else [],
                                inc=(sP if (last and nh == 1) else None))
                    wd_lastuse[bD] = dv
                    ev = pr.emit("vector", lambda E, m=m, sd=sd: E.tensor_tensor(
                        acc[:, m * 1024:(m + 1) * 1024], slotD[sd],
                        acc[:, m * 1024:(m + 1) * 1024], ALU.add),
                        waits=[(sP, dv)], inc=sV)
                    slotD_free[sd] = [(sV, ev)]
                    acc_last = ev
                    dv_last = dv
                hid_guard[0] = dv_last

        sout = xout[r0:r0 + T, :].rearrange("(m p) d -> p m d", p=128)
        stv = pr.emit("sync", lambda E, sout=sout, aview=aview: E.dma_start(sout, aview),
                      waits=[(sV, acc_last)], inc=sX, k=16)
        store_done[0] = stv

    pr.wait("sync", sX, store_done[0])



def build_l0ffn(nc, pr, env, xin, xout, W, tag, cc_bufs, npass=None, nex=None):
    T = 1024
    MP = T // 128
    NPASS = (SEQ // T) if npass is None else npass
    NSUB = SEQ // 128
    NEX = 10 if nex is None else nex

    sX = Ctr(nc, tag + "_sX")
    sXT = [Ctr(nc, f"{tag}_sXT{b}") for b in range(2)]
    sRW = Ctr(nc, tag + "_sRW")
    sUG = [Ctr(nc, f"{tag}_sUG{b}") for b in range(2)]
    sWD = [Ctr(nc, f"{tag}_sWD{b}") for b in range(2)]
    sP = Ctr(nc, tag + "_sP")
    sV = Ctr(nc, tag + "_sV")
    sA = Ctr(nc, tag + "_sA")
    sCC = Ctr(nc, tag + "_sCC")
    sCL = Ctr(nc, tag + "_sCL")

    acc = get_buf(nc, env, "b_acc", (128, 8192), F32)
    xhat = get_buf(nc, env, "b_xhat", (128, 8192), BF16)
    hid = get_buf(nc, env, "b_hid", (128, 8192), BF16)
    wu = [get_buf(nc, env, f"b_wu{b}", (128, 4096), BF16) for b in range(2)]
    wg = [get_buf(nc, env, f"b_wg{b}", (128, 4096), BF16) for b in range(2)]
    wd = [get_buf(nc, env, f"b_wd{b}", (128, 8192), BF16) for b in range(2)]
    sg = [get_buf(nc, env, f"b_sg{b}", (128, 512), BF16) for b in range(2)]
    xh = get_buf(nc, env, "b_xh", (128, 1024), BF16)
    st = get_buf(nc, env, "b_st", (128, 16), F32)
    xt = [get_buf(nc, env, f"b_xt{b}", (128, 1024), F32) for b in range(2)]
    g0T = get_buf(nc, env, tag + "_g0T", (128, 64), BF16)
    lbr = get_buf(nc, env, tag + "_lb", (1, 8), BF16)
    logits_all = get_buf(nc, env, tag + "_lga", (128, 8 * NSUB), F32)
    sel_all = get_buf(nc, env, tag + "_sel", (128, 8 * NSUB), BF16)
    sc8 = get_buf(nc, env, tag + "_sc8", (128, 8), F32)
    msc = get_buf(nc, env, tag + "_msc", (128, 8), F32)
    t1sel = get_buf(nc, env, tag + "_t1s", (128, 16), BF16)
    rr = get_buf(nc, env, tag + "_rr", (128, 8), F32)
    gs = get_buf(nc, env, tag + "_gs", (1, 16), F32)
    gsr = get_buf(nc, env, tag + "_gsr", (1, 16), F32)
    wmrow = get_buf(nc, env, tag + "_wmr", (1, 16), F32)
    wmb = get_buf(nc, env, tag + "_wmb", (128, 8), F32)
    ce_pass = get_buf(nc, env, tag + "_cep", (128, 8 * MP), F32)
    onesf_row = get_buf(nc, env, "onesf_row", (1, 128), F32)

    ps_a = env.get(("psum", "A"))
    if ps_a is None:
        ps_a = env[("psum", "A")] = nc.alloc_psum_tensor("psA", [128, 2048], F32)
    ps_d = env.get(("psum", "D"))
    if ps_d is None:
        ps_d = env[("psum", "D")] = nc.alloc_psum_tensor("psD", [128, 2048], F32)
    slotA = [ps_a[:, i * 512:(i + 1) * 512] for i in range(4)]
    slotD = [ps_d[:, i * 1024:(i + 1) * 1024] for i in range(2)]
    ps_tr = [ps_d[:, 0:64].bitcast(BF16), ps_d[:, 512:576].bitcast(BF16)]
    ps_lg = ps_d[:, 1024:1032]
    ps_gs = ps_d[0:1, 1536:1552]
    ps_wb = ps_d[:, 1792:1800]

    identbf = env[("buf", "identbf")]
    epsc = env[("buf", "epsc")]
    zeroc = env[("buf", "zeroc")]
    ones_col = env[("buf", "ones_col")]
    ones_row = env[("buf", "ones_row")]
    init = env["init"]
    init_w = (init, init.n)
    cc_in, cc_out = cc_bufs

    g0_in = W["g0T"].rearrange("(c p) e -> p c e", p=128)
    g0_out = g0T[:, :].rearrange("p (c e) -> p c e", e=8)
    pr.emit("gpsimd", lambda E: E.dma_start(g0_out, g0_in), inc=sRW, k=16)
    rw_done = pr.emit("gpsimd", lambda E: E.dma_start(lbr[:, :], W["lb"]), inc=sRW, k=16)
    pr.emit("gpsimd", lambda E: E.memset(onesf_row[:, :], 1.0), inc=init)
    init_w = (init, init.n)

    slotA_free = [[] for _ in range(4)]
    slotD_free = [[] for _ in range(2)]
    nextA = [0]

    def take_slotA():
        i = nextA[0] % 4
        nextA[0] += 1
        return i

    wu_lastuse = [0, 0]
    wd_lastuse = [0, 0]
    sg_free = [0, 0]
    hid_guard = [0]
    xt_free = [0, 0]
    tr_free = [0, 0]
    tr_war = [0]

    def emit_ln(src_ap, vwaits):
        vs1 = pr.emit("vector", lambda E: E.bn_stats(st[:, 0:6], src_ap[:, 0:512]),
                      waits=vwaits, inc=sV)
        vs2 = pr.emit("vector", lambda E: E.bn_stats(st[:, 6:12], src_ap[:, 512:1024]),
                      inc=sV)
        va = pr.emit("vector", lambda E: E.bn_aggr(st[:, 12:14], st[:, 0:12]),
                     waits=[(sV, vs2)], inc=sV)
        aa = pr.emit("scalar", lambda E: E.activation(
            st[:, 15:16], st[:, 13:14], AF.Ln, bias=epsc[:, :], scale=1.0),
            waits=[(sV, va), init_w], inc=sA)
        ab = pr.emit("scalar", lambda E: E.activation(
            st[:, 14:15], st[:, 15:16], AF.Exp, bias=zeroc[:, :], scale=-0.5),
            waits=[(sA, aa)], inc=sA)
        vx = pr.emit("vector", lambda E: E.tensor_scalar(
            xh[:, :], src_ap, st[:, 12:13], st[:, 14:15], ALU.subtract, ALU.mult),
            waits=[(sA, ab), (sP, tr_war[0])], inc=sV)
        return vx

    def emit_transposes(vx, dest, base, chunk_stride):
        last = 0
        tv = 0
        for c in range(NDC):
            b = c % 2
            tv = pr.emit("tensor", lambda E, c=c, b=b: E.transpose(
                ps_tr[b], xh[:, c * 128:(c + 1) * 128], identbf[:, :]),
                waits=[(sV, vx), (sV, tr_free[b]), init_w], inc=sP)
            last = pr.emit("vector", lambda E, c=c, b=b: E.tensor_copy(
                dest[:, base + c * chunk_stride: base + c * chunk_stride + 128],
                ps_tr[b]), waits=[(sP, tv)], inc=sV)
            tr_free[b] = last
        tr_war[0] = tv
        return last

    lg_free = []
    for ms in range(NSUB):
        b = ms % 2
        ldv = pr.emit("sync", lambda E, ms=ms, b=b: E.dma_start(
            xt[b][:, :], xin[ms * 128:(ms + 1) * 128, :]),
            waits=[(sV, xt_free[b])], inc=sXT[b], k=16)
        vx = emit_ln(xt[b][:, :], [(sXT[b], ldv)])
        xt_free[b] = vx
        xfm = emit_transposes(vx, xhat, 0, 128)
        for c in range(NDC):
            pr.emit("tensor", lambda E, c=c: E.matmul(
                ps_lg, xhat[:, c * 128:(c + 1) * 128],
                g0T[:, c * 8:(c + 1) * 8],
                start=(c == 0), stop=False),
                waits=([(sV, xfm), (sRW, rw_done)] + lg_free) if c == 0 else [])
        lgv = pr.emit("tensor", lambda E: E.matmul(
            ps_lg, ones_row[:, :], lbr[:, :], start=False, stop=True), inc=sP)
        cpv = pr.emit("vector", lambda E, ms=ms: E.tensor_copy(
            logits_all[:, ms * 8:(ms + 1) * 8], ps_lg), waits=[(sP, lgv)], inc=sV)
        lg_free = [(sV, cpv)]

    lga_done = lg_free[0][1]

    prev_t1 = 0
    gs_mm = 0
    for ms in range(NSUB):
        av = pr.emit("scalar", lambda E, ms=ms: E.activation(
            sc8[:, :], logits_all[:, ms * 8:(ms + 1) * 8], AF.Sigmoid,
            bias=zeroc[:, :]),
            waits=[(sV, lga_done if ms == 0 else 0), (sV, prev_t1)], inc=sA)
        n1 = pr.emit("vector", lambda E: E.reduce_max(
            rr[:, 0:1], sc8[:, :], axis=AX.X, negate=True), waits=[(sA, av)], inc=sV)
        m1 = pr.emit("vector", lambda E: E.tensor_scalar(
            rr[:, 2:3], rr[:, 0:1], -1.0, None, ALU.mult), waits=[(sV, n1)], inc=sV)
        e1 = pr.emit("vector", lambda E: E.tensor_scalar(
            msc[:, :], sc8[:, :], rr[:, 0:1], 0.0, ALU.add, ALU.is_ge),
            waits=[(sV, n1)], inc=sV)
        mk = pr.emit("vector", lambda E: E.scalar_tensor_tensor(
            msc[:, :], msc[:, :], -1e30, sc8[:, :], ALU.mult, ALU.add),
            waits=[(sV, e1)], inc=sV)
        n2 = pr.emit("vector", lambda E: E.reduce_max(
            rr[:, 1:2], msc[:, :], axis=AX.X, negate=True), waits=[(sV, mk)], inc=sV)
        slv = pr.emit("vector", lambda E, ms=ms: E.tensor_scalar(
            sel_all[:, ms * 8:(ms + 1) * 8], sc8[:, :], rr[:, 1:2], 0.0,
            ALU.add, ALU.is_ge), waits=[(sV, n2)], inc=sV)
        dn = pr.emit("vector", lambda E: E.scalar_tensor_tensor(
            rr[:, 3:4], rr[:, 1:2], -1.0, rr[:, 2:3], ALU.mult, ALU.add),
            waits=[(sV, n2), (sV, m1)], inc=sV)
        dn2 = pr.emit("vector", lambda E: E.tensor_scalar(
            rr[:, 3:4], rr[:, 3:4], 1e-8, None, ALU.add), waits=[(sV, dn)], inc=sV)
        rv = pr.emit("vector", lambda E: E.reciprocal(rr[:, 4:5], rr[:, 3:4]),
                     waits=[(sV, dn2)], inc=sV)
        t1 = pr.emit("vector", lambda E, ms=ms: E.scalar_tensor_tensor(
            t1sel[:, 0:8], sc8[:, :], rr[:, 4:5],
            sel_all[:, ms * 8:(ms + 1) * 8], ALU.mult, ALU.mult),
            waits=[(sV, rv), (sV, slv), (sP, gs_mm)], inc=sV)
        t1b = pr.emit("vector", lambda E, ms=ms: E.tensor_copy(
            t1sel[:, 8:16], sel_all[:, ms * 8:(ms + 1) * 8]),
            waits=[(sV, slv)], inc=sV)
        prev_t1 = t1b
        gs_mm = pr.emit("tensor", lambda E, ms=ms: E.matmul(
            ps_gs, ones_col[:, :], t1sel[:, :],
            start=(ms == 0), stop=(ms == NSUB - 1)),
            waits=[(sV, t1b), (sV, t1), init_w], inc=sP)

    gse = pr.emit("vector", lambda E: E.tensor_copy(gs[:, :], ps_gs),
                  waits=[(sP, gs_mm)], inc=sV)
    c1 = pr.emit("gpsimd", lambda E: E.dma_start(cc_in, gs[:, :]),
                 waits=[(sV, gse)], inc=sCC, k=16)
    c2 = pr.emit("gpsimd", lambda E: E.collective_compute(
        "AllReduce", ALU.add, replica_groups=[list(range(NCORES))],
        ins=[cc_in], outs=[cc_out]), waits=[(sCC, c1)], inc=sCL, k=1)
    c3 = pr.emit("gpsimd", lambda E: E.dma_start(gsr[:, :], cc_out),
                 waits=[(sCL, c2)], inc=sCC, k=16)
    w1 = pr.emit("vector", lambda E: E.tensor_scalar(
        wmrow[0:1, 8:16], gsr[0:1, 8:16], 1.0, None, ALU.max),
        waits=[(sCC, c3)], inc=sV)
    w2 = pr.emit("vector", lambda E: E.reciprocal(wmrow[0:1, 0:8], wmrow[0:1, 8:16]),
                 waits=[(sV, w1)], inc=sV)
    w3 = pr.emit("vector", lambda E: E.scalar_tensor_tensor(
        wmrow[0:1, 8:16], gsr[0:1, 0:8], 0.5, wmrow[0:1, 0:8], ALU.mult, ALU.mult),
        waits=[(sV, w2)], inc=sV)
    wbm = pr.emit("tensor", lambda E: E.matmul(
        ps_wb, onesf_row[:, :], wmrow[0:1, 8:16], start=True, stop=True),
        waits=[(sV, w3), init_w], inc=sP)
    wbv = pr.emit("vector", lambda E: E.tensor_copy(wmb[:, :], ps_wb),
                  waits=[(sP, wbm)], inc=sV)

    def load_upgate(e, buf):
        if e < 8:
            apu = W["euT"][e, :, :].rearrange("(c p) h -> p c h", p=128)
            apg = W["egT"][e, :, :].rearrange("(c p) h -> p c h", p=128)
        else:
            c0 = (e - 8) * 512
            apu = W["suT"][:, c0:c0 + 512].rearrange("(c p) h -> p c h", p=128)
            apg = W["sgT"][:, c0:c0 + 512].rearrange("(c p) h -> p c h", p=128)
        wub = wu[buf][:, :].rearrange("p (c h) -> p c h", h=512)
        wgb = wg[buf][:, :].rearrange("p (c h) -> p c h", h=512)
        pr.emit("gpsimd", lambda E: E.dma_start(wub, apu),
                waits=[(sP, wu_lastuse[buf])], inc=sUG[buf], k=16)
        return pr.emit("gpsimd", lambda E: E.dma_start(wgb, apg), inc=sUG[buf], k=16)

    def load_down(e, buf):
        if e < 8:
            ap = W["edT"][e, :, :].rearrange("(c p) d -> p c d", p=128)
        else:
            r0 = (e - 8) * 512
            ap = W["sdT"][r0:r0 + 512, :].rearrange("(c p) d -> p c d", p=128)
        wdb = wd[buf][:, 0:4096].rearrange("p (c d) -> p c d", d=1024)
        return pr.emit("gpsimd", lambda E: E.dma_start(wdb, ap),
                       waits=[(sP, wd_lastuse[buf])], inc=sWD[buf], k=16)

    store_done = [0]
    for p in range(NPASS):
        r0 = p * T
        ain = xin[r0:r0 + T, :].rearrange("(m p) d -> p m d", p=128)
        aview = acc[:, :].rearrange("p (m d) -> p m d", d=1024)
        ldx = pr.emit("sync", lambda E, ain=ain, aview=aview: E.dma_start(aview, ain),
                      waits=[(sX, store_done[0])], inc=sX, k=16)

        xhat_all = 0
        for m in range(MP):
            src = acc[:, m * 1024:(m + 1) * 1024]
            vx = emit_ln(src, [(sX, ldx)])
            xhat_all = emit_transposes(vx, xhat, m * 128, T)
        cep = 0
        for m in range(MP):
            cep = pr.emit("vector", lambda E, p=p, m=m: E.tensor_tensor(
                ce_pass[:, m * 8:(m + 1) * 8],
                sel_all[:, (p * MP + m) * 8:(p * MP + m + 1) * 8],
                wmb[:, :], ALU.mult), waits=[(sV, wbv)], inc=sV)

        acc_last = 0
        for e in range(NEX):
            b = e % 2
            ug_ready = (sUG[b], load_upgate(e, b))
            d_ready = (sWD[b], load_down(e, b))
            hid_last = 0
            for hc in range(4):
                for ts in range(T // 512):
                    siu = take_slotA()
                    uv = None
                    for dc in range(NDC):
                        uv = pr.emit("tensor", lambda E, b=b, hc=hc, dc=dc, ts=ts, siu=siu: E.matmul(
                            slotA[siu],
                            wu[b][:, dc * 512 + hc * 128: dc * 512 + (hc + 1) * 128],
                            xhat[:, dc * T + ts * 512: dc * T + (ts + 1) * 512],
                            start=(dc == 0), stop=(dc == NDC - 1)),
                            waits=([ug_ready, (sV, xhat_all)] + slotA_free[siu])
                                  if dc == 0 else [],
                            inc=(sP if dc == NDC - 1 else None))
                    wu_lastuse[b] = uv
                    sig = take_slotA()
                    gv = None
                    for dc in range(NDC):
                        gv = pr.emit("tensor", lambda E, b=b, hc=hc, dc=dc, ts=ts, sig=sig: E.matmul(
                            slotA[sig],
                            wg[b][:, dc * 512 + hc * 128: dc * 512 + (hc + 1) * 128],
                            xhat[:, dc * T + ts * 512: dc * T + (ts + 1) * 512],
                            start=(dc == 0), stop=(dc == NDC - 1)),
                            waits=slotA_free[sig] if dc == 0 else [],
                            inc=(sP if dc == NDC - 1 else None))
                    wu_lastuse[b] = gv
                    sb = (hc * (T // 512) + ts) % 2
                    av = pr.emit("scalar", lambda E, sb=sb, sig=sig: E.activation(
                        sg[sb][:, :], slotA[sig], AF.Silu, bias=zeroc[:, :]),
                        waits=[(sP, gv), (sV, sg_free[sb])], inc=sA)
                    hv = pr.emit("vector", lambda E, hc=hc, ts=ts, sb=sb, siu=siu: E.tensor_tensor(
                        hid[:, hc * T + ts * 512: hc * T + (ts + 1) * 512],
                        slotA[siu], sg[sb][:, :], ALU.mult),
                        waits=[(sP, uv), (sA, av), (sP, hid_guard[0])], inc=sV)
                    slotA_free[siu] = [(sV, hv)]
                    slotA_free[sig] = [(sV, hv)]
                    sg_free[sb] = hv
                    hid_last = hv
            dv_last = 0
            for m in range(MP):
                sd = (e * MP + m) % 2
                dv = None
                for nh in range(2):
                    for hc in range(4):
                        first = (hc == 0)
                        last = (hc == 3)
                        dv = pr.emit("tensor", lambda E, b=b, hc=hc, nh=nh, m=m, sd=sd, first=first, last=last: E.matmul(
                            slotD[sd][:, nh * 512:(nh + 1) * 512],
                            hid[:, hc * T + m * 128: hc * T + (m + 1) * 128],
                            wd[b][:, hc * 1024 + nh * 512: hc * 1024 + (nh + 1) * 512],
                            start=first, stop=last),
                            waits=([d_ready, (sV, hid_last)]
                                   + (slotD_free[sd] if nh == 0 else []))
                                  if first else [],
                            inc=(sP if (last and nh == 1) else None))
                wd_lastuse[b] = dv
                if e < 8:
                    ev = pr.emit("vector", lambda E, m=m, sd=sd, e=e: E.scalar_tensor_tensor(
                        acc[:, m * 1024:(m + 1) * 1024], slotD[sd],
                        ce_pass[:, m * 8 + e: m * 8 + e + 1],
                        acc[:, m * 1024:(m + 1) * 1024], ALU.mult, ALU.add),
                        waits=[(sP, dv), (sV, cep)], inc=sV)
                else:
                    ev = pr.emit("vector", lambda E, m=m, sd=sd: E.scalar_tensor_tensor(
                        acc[:, m * 1024:(m + 1) * 1024], slotD[sd], 0.5,
                        acc[:, m * 1024:(m + 1) * 1024], ALU.mult, ALU.add),
                        waits=[(sP, dv)], inc=sV)
                slotD_free[sd] = [(sV, ev)]
                acc_last = ev
                dv_last = dv
            hid_guard[0] = dv_last

        sout = xout[r0:r0 + T, :].rearrange("(m p) d -> p m d", p=128)
        stv = pr.emit("sync", lambda E, sout=sout, aview=aview: E.dma_start(sout, aview),
                      waits=[(sV, acc_last)], inc=sX, k=16)
        store_done[0] = stv

    pr.wait("sync", sX, store_done[0])



def build_l0attn(nc, pr, env, xin, xout, W, tag, nqb=None):
    NSUB = SEQ // 128
    NQB = NSUB if nqb is None else nqb
    SC = 1.0 / 32.0

    sX = [Ctr(nc, f"{tag}_sX{b}") for b in range(2)]
    sXO = Ctr(nc, tag + "_sXO")
    sWQ = Ctr(nc, tag + "_sWQ")
    sP = Ctr(nc, tag + "_sP")
    sV = Ctr(nc, tag + "_sV")
    sA = Ctr(nc, tag + "_sA")

    kfm = get_buf(nc, env, "a_kfm", (128, 8 * SEQ), BF16)
    vtm = get_buf(nc, env, "a_vtm", (128, NSUB * 1024), BF16)
    wk_sb = get_buf(nc, env, "a_wk", (128, 8 * 1024), BF16)
    wq_sb = wk_sb
    wv_sb = get_buf(nc, env, "a_wv", (128, 8 * 1024), BF16)
    bq_sb = get_buf(nc, env, "a_bq", (128, 8), F32)
    sbq = get_buf(nc, env, "a_sbq", (128, 8), F32)
    bk_sb = get_buf(nc, env, "a_bk", (128, 8), F32)
    bvr = get_buf(nc, env, "a_bvr", (1, 1024), F32)
    bvb = get_buf(nc, env, "a_bvb", (128, 1024), BF16)
    xfm = get_buf(nc, env, "a_xfm", (128, 8 * 512), BF16)
    xqfm = get_buf(nc, env, "a_xqfm", (128, 8 * 128), BF16)
    pt = get_buf(nc, env, "a_pt", (128, NSUB * 128), BF16)
    stats = get_buf(nc, env, "a_stats", (128, 2 * NSUB), F32)
    mx8 = get_buf(nc, env, "a_mx8", (128, 12), F32)
    negrow = get_buf(nc, env, "a_negrow", (1, 128), BF16)
    xnew = [get_buf(nc, env, "a_xnew", (128, 1024), F32)] * 2
    xt = [get_buf(nc, env, f"b_xt{b}", (128, 1024), F32) for b in range(2)]
    xh = get_buf(nc, env, "b_xh", (128, 1024), BF16)
    st = get_buf(nc, env, "b_st", (128, 16), F32)

    ps_a = env.get(("psum", "A"))
    if ps_a is None:
        ps_a = env[("psum", "A")] = nc.alloc_psum_tensor("psA", [128, 2048], F32)
    ps_d = env.get(("psum", "D"))
    if ps_d is None:
        ps_d = env[("psum", "D")] = nc.alloc_psum_tensor("psD", [128, 2048], F32)
    slotA = [ps_a[:, i * 512:(i + 1) * 512] for i in range(4)]
    slotD = [ps_d[:, i * 1024:(i + 1) * 1024] for i in range(2)]
    ps_tr = [ps_d[:, 0:64].bitcast(BF16), ps_d[:, 512:576].bitcast(BF16)]
    ps_nr = ps_d[0:1, 1024:1152]

    identbf = env[("buf", "identbf")]
    identf = env[("buf", "identf")]
    epsc = env[("buf", "epsc")]
    zeroc = env[("buf", "zeroc")]
    ones_col = env[("buf", "ones_col")]
    ones_row = env[("buf", "ones_row")]
    init = env["init"]
    init_w = (init, init.n)

    for nm, buf in (("wkT", wk_sb), ("wvT", wv_sb)):
        ap = W[nm].rearrange("(c p) j -> p c j", p=128)
        bufv = buf[:, :].rearrange("p (c j) -> p c j", j=1024)
        pr.emit("gpsimd", lambda E, bufv=bufv, ap=ap: E.dma_start(bufv, ap),
                inc=sWQ, k=16)
    pr.emit("gpsimd", lambda E: E.dma_start(bq_sb[:, :], W["bq"]), inc=sWQ, k=16)
    pr.emit("gpsimd", lambda E: E.dma_start(bk_sb[:, :], W["bk"]), inc=sWQ, k=16)
    wload = pr.emit("gpsimd", lambda E: E.dma_start(bvr[:, :], W["bv"]), inc=sWQ, k=16)
    sbqv = pr.emit("vector", lambda E: E.tensor_scalar(
        sbq[:, :], bq_sb[:, :], SC, None, ALU.mult), waits=[(sWQ, wload)], inc=sV)
    onesf_row = get_buf(nc, env, "onesf_row", (1, 128), F32)
    ofr = pr.emit("gpsimd", lambda E: E.memset(onesf_row[:, :], 1.0), inc=init)
    bvm = None
    for nh in range(2):
        bvm = pr.emit("tensor", lambda E, nh=nh: E.matmul(
            slotD[0][:, nh * 512:(nh + 1) * 512], onesf_row[:, :],
            bvr[0:1, nh * 512:(nh + 1) * 512], start=True, stop=True),
            waits=[(sWQ, wload), (init, ofr)], inc=sP)
    bvv = pr.emit("vector", lambda E: E.tensor_copy(bvb[:, :], slotD[0]),
                  waits=[(sP, bvm)], inc=sV)

    slotA_free = [[] for _ in range(4)]
    slotD_free = [[(sV, bvv)], []]
    nextA = [0]

    def take_slotA():
        i = nextA[0] % 4
        nextA[0] += 1
        return i

    tr_free = [0, 0]
    tr_war = [0]
    xt_free = [0, 0]

    def emit_ln(src_ap, vwaits, si):
        vs1 = pr.emit("vector", lambda E: E.bn_stats(st[:, 0:6], src_ap[:, 0:512]),
                      waits=vwaits, inc=sV)
        vs2 = pr.emit("vector", lambda E: E.bn_stats(st[:, 6:12], src_ap[:, 512:1024]),
                      inc=sV)
        va = pr.emit("vector", lambda E, si=si: E.bn_aggr(
            stats[:, 2 * si:2 * si + 2], st[:, 0:12]), waits=[(sV, vs2)], inc=sV)
        aa = pr.emit("scalar", lambda E, si=si: E.activation(
            st[:, 15:16], stats[:, 2 * si + 1:2 * si + 2], AF.Ln,
            bias=epsc[:, :], scale=1.0), waits=[(sV, va), init_w], inc=sA)
        ab = pr.emit("scalar", lambda E, si=si: E.activation(
            stats[:, 2 * si + 1:2 * si + 2], st[:, 15:16], AF.Exp,
            bias=zeroc[:, :], scale=-0.5), waits=[(sA, aa)], inc=sA)
        vx = pr.emit("vector", lambda E, si=si: E.tensor_scalar(
            xh[:, :], src_ap, stats[:, 2 * si:2 * si + 1],
            stats[:, 2 * si + 1:2 * si + 2], ALU.subtract, ALU.mult),
            waits=[(sA, ab), (sP, tr_war[0])], inc=sV)
        return vx

    def emit_transposes(vx, dest, base, chunk_stride):
        last = 0
        tv = 0
        for c in range(NDC):
            b = c % 2
            tv = pr.emit("tensor", lambda E, c=c, b=b: E.transpose(
                ps_tr[b], xh[:, c * 128:(c + 1) * 128], identbf[:, :]),
                waits=[(sV, vx), (sV, tr_free[b]), init_w], inc=sP)
            last = pr.emit("vector", lambda E, c=c, b=b: E.tensor_copy(
                dest[:, base + c * chunk_stride: base + c * chunk_stride + 128],
                ps_tr[b]), waits=[(sP, tv)], inc=sV)
            tr_free[b] = last
        tr_war[0] = tv
        return last

    kfm_done = 0
    vtm_done = 0
    wk_lastuse = [0]
    for g in range(8):
        xfm_last = 0
        for ms in range(4):
            si = g * 4 + ms
            b = si % 2
            ldv = pr.emit("sync", lambda E, si=si, b=b: E.dma_start(
                xt[b][:, :], xin[si * 128:(si + 1) * 128, :]),
                waits=[(sV, xt_free[b])], inc=sX[b], k=16)
            vx = emit_ln(xt[b][:, :], [(sX[b], ldv)], si)
            xt_free[b] = vx
            xfm_last = emit_transposes(vx, xfm, ms * 128, 512)
        for jc in range(8):
            si_ = take_slotA()
            kv = None
            for dc in range(NDC):
                kv = pr.emit("tensor", lambda E, jc=jc, dc=dc, si_=si_: E.matmul(
                    slotA[si_],
                    wk_sb[:, dc * 1024 + jc * 128: dc * 1024 + (jc + 1) * 128],
                    xfm[:, dc * 512:(dc + 1) * 512],
                    start=(dc == 0), stop=(dc == NDC - 1)),
                    waits=([(sV, xfm_last), (sWQ, wload)] + slotA_free[si_])
                          if dc == 0 else [],
                    inc=(sP if dc == NDC - 1 else None))
            kev = pr.emit("scalar", lambda E, jc=jc, g=g, si_=si_: E.activation(
                kfm[:, jc * SEQ + g * 512: jc * SEQ + (g + 1) * 512],
                slotA[si_], AF.Identity, bias=bk_sb[:, jc:jc + 1], scale=1.0),
                waits=[(sP, kv)], inc=sA)
            slotA_free[si_] = [(sA, kev)]
            kfm_done = kev
            wk_lastuse[0] = kv
        for ms in range(4):
            si = g * 4 + ms
            sd = si % 2
            vv = None
            for nh in range(2):
                for dc in range(NDC):
                    first = (dc == 0)
                    vv = pr.emit("tensor", lambda E, ms=ms, nh=nh, dc=dc, sd=sd, first=first: E.matmul(
                        slotD[sd][:, nh * 512:(nh + 1) * 512],
                        xfm[:, dc * 512 + ms * 128: dc * 512 + (ms + 1) * 128],
                        wv_sb[:, dc * 1024 + nh * 512: dc * 1024 + (nh + 1) * 512],
                        start=first, stop=(dc == NDC - 1)),
                        waits=([(sV, xfm_last)] + (slotD_free[sd] if nh == 0 else []))
                              if first else [],
                        inc=(sP if (dc == NDC - 1 and nh == 1) else None))
            vev = pr.emit("vector", lambda E, si=si, sd=sd: E.tensor_tensor(
                vtm[:, si * 1024:(si + 1) * 1024], slotD[sd], bvb[:, :], ALU.add),
                waits=[(sP, vv), (sV, bvv)], inc=sV)
            slotD_free[sd] = [(sV, vev)]
            vtm_done = vev

    wq_ap = W["wqT"].rearrange("(c p) j -> p c j", p=128)
    wq_view = wq_sb[:, :].rearrange("p (c j) -> p c j", j=1024)
    wq_done = pr.emit("gpsimd", lambda E: E.dma_start(wq_view, wq_ap),
                      waits=[(sP, wk_lastuse[0])], inc=sWQ, k=16)
    st_done = 0
    xnew_free = [0, 0]
    for qb in range(NQB):
        b = qb % 2
        ldv = pr.emit("sync", lambda E, qb=qb, b=b: E.dma_start(
            xt[b][:, :], xin[qb * 128:(qb + 1) * 128, :]),
            waits=[(sV, xt_free[b])], inc=sX[b], k=16)
        vx = pr.emit("vector", lambda E, qb=qb, b=b: E.tensor_scalar(
            xh[:, :], xt[b][:, :], stats[:, 2 * qb:2 * qb + 1],
            stats[:, 2 * qb + 1:2 * qb + 2], ALU.subtract, ALU.mult),
            waits=[(sX[b], ldv), (sP, tr_war[0])], inc=sV)
        xfm_last = emit_transposes(vx, xfm, 0, 128)
        xq_done = 0
        for jc in range(8):
            si_ = take_slotA()
            qv = None
            for dc in range(NDC):
                qv = pr.emit("tensor", lambda E, jc=jc, dc=dc, si_=si_: E.matmul(
                    slotA[si_][:, 0:128],
                    wq_sb[:, dc * 1024 + jc * 128: dc * 1024 + (jc + 1) * 128],
                    xfm[:, dc * 128:(dc + 1) * 128],
                    start=(dc == 0), stop=(dc == NDC - 1)),
                    waits=([(sV, xfm_last), (sV, sbqv), (sWQ, wq_done)]
                           + slotA_free[si_]) if dc == 0 else [],
                    inc=(sP if dc == NDC - 1 else None))
            qev = pr.emit("scalar", lambda E, jc=jc, si_=si_: E.activation(
                xqfm[:, jc * 128:(jc + 1) * 128], slotA[si_][:, 0:128],
                AF.Identity, bias=sbq[:, jc:jc + 1], scale=SC),
                waits=[(sP, qv)], inc=sA)
            slotA_free[si_] = [(sA, qev)]
            xq_done = qev
        for kb in range(8):
            si_ = take_slotA()
            sv1 = None
            for jc in range(8):
                sv1 = pr.emit("tensor", lambda E, kb=kb, jc=jc, si_=si_: E.matmul(
                    slotA[si_],
                    xqfm[:, jc * 128:(jc + 1) * 128],
                    kfm[:, jc * SEQ + kb * 512: jc * SEQ + (kb + 1) * 512],
                    start=(jc == 0), stop=(jc == NDC - 1)),
                    waits=([(sA, xq_done), (sA, kfm_done)] + slotA_free[si_])
                          if jc == 0 else [],
                    inc=(sP if jc == NDC - 1 else None))
            mv = pr.emit("vector", lambda E, kb=kb, si_=si_: E.reduce_max(
                mx8[:, kb:kb + 1], slotA[si_], axis=AX.X), waits=[(sP, sv1)], inc=sV)
            slotA_free[si_] = [(sV, mv)]
        nmv = pr.emit("vector", lambda E: E.reduce_max(
            mx8[:, 8:9], mx8[:, 0:8], axis=AX.X, negate=True),
            waits=[(sV, mv)], inc=sV)
        ntv = pr.emit("tensor", lambda E: E.transpose(
            ps_nr, mx8[:, 8:9], identf[:, :]),
            waits=[(sV, nmv), init_w], inc=sP)
        nrv = pr.emit("vector", lambda E: E.tensor_copy(negrow[:, :], ps_nr),
                      waits=[(sP, ntv)], inc=sV)
        sd = qb % 2
        den_slot = 3
        pt_rot = [0]
        pv_last = None
        den_last = None
        for pk in range(NSUB):
            si_ = pt_rot[0] % 3
            pt_rot[0] += 1
            s2 = None
            for jc in range(8):
                s2 = pr.emit("tensor", lambda E, pk=pk, jc=jc, si_=si_: E.matmul(
                    slotA[si_][:, 0:128],
                    kfm[:, jc * SEQ + pk * 128: jc * SEQ + (pk + 1) * 128],
                    xqfm[:, jc * 128:(jc + 1) * 128],
                    start=(jc == 0), stop=False),
                    waits=([(sV, nrv)] + slotA_free[si_]) if jc == 0 else [])
            s2 = pr.emit("tensor", lambda E, si_=si_: E.matmul(
                slotA[si_][:, 0:128], ones_row[:, :], negrow[:, :],
                start=False, stop=True), inc=sP)
            pev = pr.emit("scalar", lambda E, pk=pk, si_=si_: E.activation(
                pt[:, pk * 128:(pk + 1) * 128], slotA[si_][:, 0:128], AF.Exp,
                bias=zeroc[:, :]), waits=[(sP, s2)], inc=sA)
            slotA_free[si_] = [(sA, pev)]
            for nh in range(2):
                pv_last = pr.emit("tensor", lambda E, pk=pk, nh=nh, sd=sd: E.matmul(
                    slotD[sd][:, nh * 512:(nh + 1) * 512],
                    pt[:, pk * 128:(pk + 1) * 128],
                    vtm[:, pk * 1024 + nh * 512: pk * 1024 + (nh + 1) * 512],
                    start=(pk == 0), stop=(pk == NSUB - 1)),
                    waits=([(sA, pev), (sV, vtm_done)]
                           + (slotD_free[sd] if (pk == 0 and nh == 0) else [])),
                    inc=(sP if (pk == NSUB - 1 and nh == 1) else None))
            den_last = pr.emit("tensor", lambda E, pk=pk, den_slot=den_slot: E.matmul(
                slotA[den_slot][:, 0:1], pt[:, pk * 128:(pk + 1) * 128],
                ones_col[:, :], start=(pk == 0), stop=(pk == NSUB - 1)),
                waits=(slotA_free[den_slot] + [init_w]) if pk == 0 else [],
                inc=(sP if pk == NSUB - 1 else None))
        rvv = pr.emit("vector", lambda E, den_slot=den_slot: E.reciprocal(
            mx8[:, 9:10], slotA[den_slot][:, 0:1]), waits=[(sP, den_last)], inc=sV)
        slotA_free[den_slot] = [(sV, rvv)]
        env_b = 0
        epi = pr.emit("vector", lambda E, sd=sd, b=b, env_b=env_b: E.scalar_tensor_tensor(
            xnew[env_b][:, :], slotD[sd], mx8[:, 9:10], xt[b][:, :],
            ALU.mult, ALU.add),
            waits=[(sP, pv_last), (sV, rvv), (sXO, xnew_free[env_b])], inc=sV)
        slotD_free[sd] = [(sV, epi)]
        xt_free[b] = epi
        st_done = pr.emit("sync", lambda E, qb=qb, env_b=env_b: E.dma_start(
            xout[qb * 128:(qb + 1) * 128, :], xnew[env_b][:, :]),
            waits=[(sV, epi)], inc=sXO, k=16)
        xnew_free[env_b] = st_done

    pr.wait("sync", sXO, st_done)



def build_l1attn(nc, pr, env, xin, xout, W, tag, nheads=None):
    NH = 16 if nheads is None else nheads
    SCQ = 0.125

    sX = [Ctr(nc, f"{tag}_sX{b}") for b in range(2)]
    sXO = Ctr(nc, tag + "_sXO")
    sWQ = Ctr(nc, tag + "_sWQ")
    sP = Ctr(nc, tag + "_sP")
    sV = Ctr(nc, tag + "_sV")
    sA = Ctr(nc, tag + "_sA")

    wq_sb = get_buf(nc, env, "h_wq", (128, 8 * 1024), BF16)
    wk_sb = get_buf(nc, env, "h_wk", (128, 8 * 1024), BF16)
    wv_sb = get_buf(nc, env, "h_wv", (128, 8 * 1024), BF16)
    wo_lo = get_buf(nc, env, "h_wo", (64, 16 * 1024), BF16)
    xq_lo = get_buf(nc, env, "h_xq", (64, 4096), BF16)
    xk_lo = get_buf(nc, env, "h_xk", (64, 4096), BF16)
    xv = get_buf(nc, env, "h_xv", (128, 2 * 1040), BF16)
    ptb = [get_buf(nc, env, f"h_pt{b}", (128, 2048), BF16) for b in range(2)]
    scr = get_buf(nc, env, "h_scr", (128, 2048), F32)
    ao = get_buf(nc, env, "h_ao", (64, 4096), BF16)
    rb_sb = get_buf(nc, env, "h_rb", (64, 512), BF16)
    xhfm = get_buf(nc, env, "h_xfm", (128, 8 * 256), BF16)
    negrow = get_buf(nc, env, "h_neg", (1, 4096), BF16)
    mxh = get_buf(nc, env, "h_mx", (128, 12), F32)
    ones_all = get_buf(nc, env, "h_ones", (128, 128), BF16)
    xnew = get_buf(nc, env, "a_xnew", (128, 1024), F32)
    xt = [get_buf(nc, env, f"b_xt{b}", (128, 1024), F32) for b in range(2)]
    xh = get_buf(nc, env, "b_xh", (128, 1024), BF16)
    st = get_buf(nc, env, "b_st", (128, 16), F32)
    stats = get_buf(nc, env, "a_stats", (128, 2 * 32), F32)

    ps_a = env.get(("psum", "A"))
    if ps_a is None:
        ps_a = env[("psum", "A")] = nc.alloc_psum_tensor("psA", [128, 2048], F32)
    ps_d = env.get(("psum", "D"))
    if ps_d is None:
        ps_d = env[("psum", "D")] = nc.alloc_psum_tensor("psD", [128, 2048], F32)
    slotA = [ps_a[:, i * 512:(i + 1) * 512] for i in range(4)]
    ps_out = ps_d[0:65, 0:2048]
    ps_wo = [ps_d[:, i * 1024:(i + 1) * 1024] for i in range(2)]
    ps_tr = [ps_d[:, 0:64].bitcast(BF16), ps_d[:, 512:576].bitcast(BF16)]
    ps_nr = ps_d[0:1, 1024:1152]

    identbf = env[("buf", "identbf")]
    identf = env[("buf", "identf")]
    epsc = env[("buf", "epsc")]
    zeroc = env[("buf", "zeroc")]
    ones_row = env[("buf", "ones_row")]
    init = env["init"]

    for nm, buf in (("wqT", wq_sb), ("wkT", wk_sb), ("wvT", wv_sb)):
        ap = W[nm].rearrange("(c p) j -> p c j", p=128)
        bufv = buf[:, :].rearrange("p (c j) -> p c j", j=1024)
        pr.emit("gpsimd", lambda E, bufv=bufv, ap=ap: E.dma_start(bufv, ap),
                inc=sWQ, k=16)
    wload = pr.emit("gpsimd", lambda E: E.dma_start(
        wo_lo[:, :].rearrange("p (c j) -> p c j", j=1024),
        W["woLo"].rearrange("p (c j) -> p c j", j=1024)), inc=sWQ, k=16)
    pr.emit("gpsimd", lambda E: E.memset(ones_all[:, :], 1.0), inc=init)
    xvon = pr.emit("gpsimd", lambda E: E.memset(xv[:, :], 1.0), inc=init)
    init_w = (init, init.n)

    slotA_free = [[] for _ in range(4)]
    psout_free = []
    pswo_free = [[], []]
    nextA = [0]

    def take_slotA():
        i = nextA[0] % 4
        nextA[0] += 1
        return i

    tr_free = [0, 0]
    tr_war = [0]
    xt_free = [0, 0]
    xnew_free = [0]
    xv_war = [0]
    ao_war = [0]
    pt_free = [0, 0]
    neg_war = [0]
    rb_free = [0]

    def emit_ln(src_ap, vwaits, si):
        vs1 = pr.emit("vector", lambda E: E.bn_stats(st[:, 0:6], src_ap[:, 0:512]),
                      waits=vwaits, inc=sV)
        vs2 = pr.emit("vector", lambda E: E.bn_stats(st[:, 6:12], src_ap[:, 512:1024]),
                      inc=sV)
        va = pr.emit("vector", lambda E, si=si: E.bn_aggr(
            stats[:, 2 * si:2 * si + 2], st[:, 0:12]), waits=[(sV, vs2)], inc=sV)
        aa = pr.emit("scalar", lambda E, si=si: E.activation(
            st[:, 15:16], stats[:, 2 * si + 1:2 * si + 2], AF.Ln,
            bias=epsc[:, :], scale=1.0), waits=[(sV, va), init_w], inc=sA)
        ab = pr.emit("scalar", lambda E, si=si: E.activation(
            stats[:, 2 * si + 1:2 * si + 2], st[:, 15:16], AF.Exp,
            bias=zeroc[:, :], scale=-0.5), waits=[(sA, aa)], inc=sA)
        vx = pr.emit("vector", lambda E, si=si: E.tensor_scalar(
            xh[:, :], src_ap, stats[:, 2 * si:2 * si + 1],
            stats[:, 2 * si + 1:2 * si + 2], ALU.subtract, ALU.mult),
            waits=[(sA, ab), (sP, tr_war[0])], inc=sV)
        return vx

    st_done = 0
    for hh in range(NH):
        xfm_last = 0
        for ms in range(2):
            si = 2 * hh + ms
            b = ms
            ldv = pr.emit("sync", lambda E, si=si, b=b: E.dma_start(
                xt[b][:, :], xin[si * 128:(si + 1) * 128, :]),
                waits=[(sV, xt_free[b])], inc=sX[b], k=16)
            vx = emit_ln(xt[b][:, :], [(sX[b], ldv)], si % 32)
            for c in range(NDC):
                tb = c % 2
                tv = pr.emit("tensor", lambda E, c=c, tb=tb: E.transpose(
                    ps_tr[tb], xh[:, c * 128:(c + 1) * 128], identbf[:, :]),
                    waits=[(sV, vx), (sV, tr_free[tb]), init_w], inc=sP)
                xfm_last = pr.emit("vector", lambda E, c=c, tb=tb, ms=ms: E.tensor_copy(
                    xhfm[:, c * 256 + ms * 128: c * 256 + (ms + 1) * 128], ps_tr[tb]),
                    waits=[(sP, tv)], inc=sV)
                tr_free[tb] = xfm_last
            tr_war[0] = tv

        xq_done = 0
        for which, dst, wsb, scq in ((0, xq_lo, wq_sb, SCQ), (1, xk_lo, wk_sb, 1.0)):
            for cs in range(16):
                si_ = take_slotA()
                qv = None
                for dc in range(NDC):
                    qv = pr.emit("tensor", lambda E, cs=cs, dc=dc, si_=si_, wsb=wsb: E.matmul(
                        slotA[si_][0:64, 0:256],
                        wsb[:, dc * 1024 + cs * 64: dc * 1024 + (cs + 1) * 64],
                        xhfm[:, dc * 256:(dc + 1) * 256],
                        start=(dc == 0), stop=(dc == NDC - 1)),
                        waits=([(sV, xfm_last), (sWQ, wload)] + slotA_free[si_])
                              if dc == 0 else [],
                        inc=(sP if dc == NDC - 1 else None))
                qev = pr.emit("scalar", lambda E, cs=cs, si_=si_, dst=dst, scq=scq: E.activation(
                    dst[0:64, cs * 256:(cs + 1) * 256], slotA[si_][0:64, 0:256],
                    AF.Identity, bias=zeroc[0:64, :], scale=scq),
                    waits=[(sP, qv)], inc=sA)
                slotA_free[si_] = [(sA, qev)]
                xq_done = qev

        xv_done = 0
        for ms in range(2):
            sd = ms
            vv = None
            for nh in range(2):
                for dc in range(NDC):
                    first = (dc == 0)
                    vv = pr.emit("tensor", lambda E, ms=ms, nh=nh, dc=dc, sd=sd, first=first: E.matmul(
                        ps_wo[sd][:, nh * 512:(nh + 1) * 512],
                        xhfm[:, dc * 256 + ms * 128: dc * 256 + (ms + 1) * 128],
                        wv_sb[:, dc * 1024 + nh * 512: dc * 1024 + (nh + 1) * 512],
                        start=first, stop=(dc == NDC - 1)),
                        waits=([(sV, xfm_last), (sP, xv_war[0])]
                               + (pswo_free[sd] if nh == 0 else []))
                              if first else [],
                        inc=(sP if (dc == NDC - 1 and nh == 1) else None))
            vev = pr.emit("vector", lambda E, ms=ms, sd=sd: E.tensor_copy(
                xv[:, ms * 1040: ms * 1040 + 1040].rearrange(
                    "p (s w) -> p s w", w=65)[:, :, 0:64],
                ps_wo[sd][:, :].rearrange("p (s w) -> p s w", w=64)),
                waits=[(sP, vv), (init, xvon)], inc=sV)
            pswo_free[sd] = [(sV, vev)]
            xv_done = vev

        nrv = 0
        for u in range(32):
            cq, qh = u // 2, u % 2
            mvs = None
            for kb in range(8):
                si_ = take_slotA()
                s1 = pr.emit("tensor", lambda E, cq=cq, qh=qh, kb=kb, si_=si_: E.matmul(
                    slotA[si_],
                    xq_lo[0:64, cq * 256 + qh * 128: cq * 256 + (qh + 1) * 128],
                    xk_lo[0:64, kb * 512:(kb + 1) * 512],
                    start=True, stop=True),
                    waits=[(sA, xq_done)] + slotA_free[si_], inc=sP)
                mvs = pr.emit("vector", lambda E, kb=kb, si_=si_: E.reduce_max(
                    mxh[:, kb:kb + 1], slotA[si_], axis=AX.X),
                    waits=[(sP, s1)], inc=sV)
                slotA_free[si_] = [(sV, mvs)]
            nmv = pr.emit("vector", lambda E: E.reduce_max(
                mxh[:, 8:9], mxh[:, 0:8], axis=AX.X, negate=True),
                waits=[(sV, mvs)], inc=sV)
            ntv = pr.emit("tensor", lambda E: E.transpose(
                ps_nr, mxh[:, 8:9], identf[:, :]),
                waits=[(sV, nmv), init_w], inc=sP)
            nrv = pr.emit("vector", lambda E, u=u: E.tensor_copy(
                negrow[0:1, u * 128:(u + 1) * 128], ps_nr),
                waits=[(sP, ntv), (sP, neg_war[0])], inc=sV)

        aov_last = 0
        for half in range(2):
            pv_last = None
            s2b_last = 0
            for w in range(32):
                ck, kh = w // 2, w % 2
                pb = w % 2
                exv = [None] * 4
                for pqs in range(4):
                    si_ = take_slotA()
                    pr.emit("tensor", lambda E, w=w, pqs=pqs, si_=si_, half=half: E.matmul(
                        slotA[si_],
                        xk_lo[0:64, w * 128:(w + 1) * 128],
                        xq_lo[0:64, half * 2048 + pqs * 512: half * 2048 + (pqs + 1) * 512],
                        start=True, stop=False),
                        waits=[(sV, nrv)] + slotA_free[si_])
                    s2 = pr.emit("tensor", lambda E, pqs=pqs, si_=si_, half=half: E.matmul(
                        slotA[si_], ones_row[:, :],
                        negrow[0:1, half * 2048 + pqs * 512: half * 2048 + (pqs + 1) * 512],
                        start=False, stop=True), inc=sP)
                    s2b_last = s2
                    ex = pr.emit("scalar", lambda E, pqs=pqs, si_=si_, pb=pb: E.activation(
                        ptb[pb][:, pqs * 512:(pqs + 1) * 512], slotA[si_], AF.Exp,
                        bias=zeroc[:, :]),
                        waits=[(sP, s2), (sP, pt_free[pb])], inc=sA)
                    slotA_free[si_] = [(sA, ex)]
                    exv[pqs] = ex
                for pqs in range(4):
                    pv_last = pr.emit("tensor", lambda E, w=w, ck=ck, kh=kh, pqs=pqs, pb=pb: E.matmul(
                        ps_out[0:65, pqs * 512:(pqs + 1) * 512],
                        xv[:, kh * 1040 + ck * 65: kh * 1040 + (ck + 1) * 65],
                        ptb[pb][:, pqs * 512:(pqs + 1) * 512],
                        start=(w == 0), stop=(w == 31)),
                        waits=([(sA, exv[pqs]), (sV, xv_done)]
                               + (psout_free if (w == 0 and pqs == 0) else [])),
                        inc=(sP if (w == 31 and pqs == 3) else None))
                pt_free[pb] = pv_last if pv_last is not None else pt_free[pb]
            neg_war[0] = s2b_last
            rvv = pr.emit("vector", lambda E: E.reciprocal(
                scr[64:65, 0:2048], ps_out[64:65, :]), waits=[(sP, pv_last)], inc=sV)
            rbc = pr.emit("vector", lambda E: E.tensor_copy(
                ptb[0][64:65, 0:2048], scr[64:65, 0:2048]),
                waits=[(sV, rvv), (sP, pt_free[0])], inc=sV)
            aolast = 0
            for pqs in range(4):
                si_ = take_slotA()
                bc = pr.emit("tensor", lambda E, pqs=pqs, si_=si_: E.matmul(
                    slotA[si_][0:64, 0:512], ones_all[64:65, 0:64],
                    ptb[0][64:65, pqs * 512:(pqs + 1) * 512], start=True, stop=True),
                    waits=[(sV, rbc), init_w] + slotA_free[si_], inc=sP)
                pt_free[0] = bc
                rbv = pr.emit("vector", lambda E, si_=si_: E.tensor_copy(
                    rb_sb[0:64, :], slotA[si_][0:64, 0:512]),
                    waits=[(sP, bc), (sV, rb_free[0])], inc=sV)
                aov = pr.emit("vector", lambda E, half=half, pqs=pqs: E.tensor_tensor(
                    ao[0:64, half * 2048 + pqs * 512: half * 2048 + (pqs + 1) * 512],
                    ps_out[0:64, pqs * 512:(pqs + 1) * 512], rb_sb[0:64, :], ALU.mult),
                    waits=[(sV, rbv), (sP, ao_war[0])], inc=sV)
                slotA_free[si_] = [(sV, aov)]
                rb_free[0] = aov
                aolast = aov
            psout_free = [(sV, aolast)]
            aov_last = aolast

        for ms in range(2):
            sd = ms
            wv_ = None
            for cs in range(16):
                for nh in range(2):
                    first = (cs == 0)
                    wv_ = pr.emit("tensor", lambda E, ms=ms, cs=cs, nh=nh, sd=sd, first=first: E.matmul(
                        ps_wo[sd][:, nh * 512:(nh + 1) * 512],
                        ao[0:64, cs * 256 + ms * 128: cs * 256 + ms * 128 + 128],
                        wo_lo[0:64, cs * 1024 + nh * 512: cs * 1024 + (nh + 1) * 512],
                        start=first, stop=(cs == 15)),
                        waits=([(sV, aov_last), (sWQ, wload)]
                               + (pswo_free[sd] if nh == 0 else []))
                              if first else [],
                        inc=(sP if (cs == 15 and nh == 1) else None))
            ao_war[0] = wv_
            xv_war[0] = wv_
            epi = pr.emit("vector", lambda E, ms=ms, sd=sd: E.tensor_tensor(
                xnew[:, :], ps_wo[sd], xt[ms][:, :], ALU.add),
                waits=[(sP, wv_), (sXO, xnew_free[0])], inc=sV)
            stq = 2 * hh + ms
            st_done = pr.emit("sync", lambda E, stq=stq: E.dma_start(
                xout[stq * 128:(stq + 1) * 128, :], xnew[:, :]),
                waits=[(sV, epi)], inc=sXO, k=16)
            xnew_free[0] = st_done
            pswo_free[sd] = [(sV, epi)]
            xt_free[ms] = epi

    pr.wait("sync", sXO, st_done)



def _l1ffn_dram(nc):
    return {
        "rw": nc.dram_tensor("l1_rw", [1024, 8], F32, kind="ExternalInput").ap(),
        "up": nc.dram_tensor("l1_up", [7, 1024, 4096], F32, kind="ExternalInput").ap(),
        "gate": nc.dram_tensor("l1_gate", [7, 1024, 4096], F32, kind="ExternalInput").ap(),
        "down": nc.dram_tensor("l1_down", [7, 4096, 1024], F32, kind="ExternalInput").ap(),
        "sup": nc.dram_tensor("l1_sup", [1024, 4096], F32, kind="ExternalInput").ap(),
        "sgate": nc.dram_tensor("l1_sgate", [1024, 4096], F32, kind="ExternalInput").ap(),
        "sdown": nc.dram_tensor("l1_sdown", [4096, 1024], F32, kind="ExternalInput").ap(),
    }


def _l1ffn_arrays(inputs):
    c = np.ascontiguousarray
    return {
        "l1_rw": c(inputs["L1_ffn_router_w"]),
        "l1_up": c(inputs["L1_ffn_up_w"]),
        "l1_gate": c(inputs["L1_ffn_gate_w"]),
        "l1_down": c(inputs["L1_ffn_down_w"]),
        "l1_sup": c(inputs["L1_ffn_shared_up_w"]),
        "l1_sgate": c(inputs["L1_ffn_shared_gate_w"]),
        "l1_sdown": c(inputs["L1_ffn_shared_down_w"]),
    }


def build_l1ffn_program(npass=None, ne=None):
    nc = bass.Bass("TRN2", target_bir_lowering=False)
    xin = nc.dram_tensor("xin", [SEQ, D], F32, kind="ExternalInput").ap()
    xout = nc.dram_tensor("xout", [SEQ, D], F32, kind="ExternalOutput").ap()
    W = _l1ffn_dram(nc)
    env = {}
    pr = Prog(nc)
    with nc.Block() as block:
        common_init(nc, pr, env)
        build_l1ffn(nc, pr, env, xin, xout, W, "f", npass=npass, ne=ne)
        pr.run_block(block)
    return nc


_PROGS = {}


def _get_progs():
    if not _PROGS:
        _PROGS["l0ffn"] = build_l0ffn_program()
        _PROGS["l0att"] = build_l0attn_program()
        _PROGS["l1ffn"] = build_l1ffn_program()
        _PROGS["l1att"] = build_l1attn_program()
    return _PROGS


def kernel(**inputs):
    inputs = {k: np.asarray(v, dtype=np.float32) for k, v in inputs.items()}
    progs = _get_progs()
    X = np.ascontiguousarray(inputs["L0_attn_x"])
    a_l0f = _l0ffn_arrays(inputs)
    a_l0a = _l0attn_arrays(inputs)
    a_l1f = _l1ffn_arrays(inputs)
    a_l1a = _l1attn_arrays(inputs)
    xs = [np.ascontiguousarray(X[c]) for c in range(NCORES)]

    def launch(prog, arrs):
        in_maps = [dict(arrs, xin=xs[c]) for c in range(NCORES)]
        res = run_bass_kernel_spmd(prog, in_maps, core_ids=list(range(NCORES)))
        for c in range(NCORES):
            xs[c] = np.ascontiguousarray(res.results[c]["xout"])

    launch(progs["l0ffn"], a_l0f)
    launch(progs["l0att"], a_l0a)
    launch(progs["l0ffn"], a_l0f)
    launch(progs["l1ffn"], a_l1f)
    launch(progs["l1att"], a_l1a)
    launch(progs["l1ffn"], a_l1f)
    return np.stack(xs)


def _l0ffn_dram(nc):
    return {
        "g0T": nc.dram_tensor("l0_g0T", [1024, 8], F32, kind="ExternalInput").ap(),
        "lb": nc.dram_tensor("l0_lb", [1, 8], F32, kind="ExternalInput").ap(),
        "egT": nc.dram_tensor("l0_egT", [8, 1024, 512], F32, kind="ExternalInput").ap(),
        "euT": nc.dram_tensor("l0_euT", [8, 1024, 512], F32, kind="ExternalInput").ap(),
        "edT": nc.dram_tensor("l0_edT", [8, 512, 1024], F32, kind="ExternalInput").ap(),
        "sgT": nc.dram_tensor("l0_sgT", [1024, 1024], F32, kind="ExternalInput").ap(),
        "suT": nc.dram_tensor("l0_suT", [1024, 1024], F32, kind="ExternalInput").ap(),
        "sdT": nc.dram_tensor("l0_sdT", [1024, 1024], F32, kind="ExternalInput").ap(),
    }


def _l0ffn_arrays(inputs):
    c = np.ascontiguousarray
    return {
        "l0_g0T": c(inputs["L0_ffn_gate_w"].T),
        "l0_lb": c(inputs["L0_ffn_lb_bias"].reshape(1, 8)),
        "l0_egT": c(inputs["L0_ffn_expert_gate_w"].transpose(0, 2, 1)),
        "l0_euT": c(inputs["L0_ffn_expert_up_w"].transpose(0, 2, 1)),
        "l0_edT": c(inputs["L0_ffn_expert_down_w"].transpose(0, 2, 1)),
        "l0_sgT": c(inputs["L0_ffn_shared_gate_w"].T),
        "l0_suT": c(inputs["L0_ffn_shared_up_w"].T),
        "l0_sdT": c(inputs["L0_ffn_shared_down_w"].T),
    }


def build_l0ffn_program(npass=None, nex=None):
    nc = bass.Bass("TRN2", target_bir_lowering=False)
    xin = nc.dram_tensor("xin", [SEQ, D], F32, kind="ExternalInput").ap()
    xout = nc.dram_tensor("xout", [SEQ, D], F32, kind="ExternalOutput").ap()
    W = _l0ffn_dram(nc)
    cc_in = nc.dram_tensor("g_cc_in", [1, 16], F32).ap()
    cc_out = nc.dram_tensor("g_cc_out", [1, 16], F32).ap()
    env = {}
    pr = Prog(nc)
    with nc.Block() as block:
        common_init(nc, pr, env)
        build_l0ffn(nc, pr, env, xin, xout, W, "g", (cc_in, cc_out),
                    npass=npass, nex=nex)
        pr.run_block(block)
    return nc


def _l0attn_dram(nc):
    return {
        "wqT": nc.dram_tensor("a0_wqT", [1024, 1024], F32, kind="ExternalInput").ap(),
        "wkT": nc.dram_tensor("a0_wkT", [1024, 1024], F32, kind="ExternalInput").ap(),
        "wvT": nc.dram_tensor("a0_wvT", [1024, 1024], F32, kind="ExternalInput").ap(),
        "bq": nc.dram_tensor("a0_bq", [128, 8], F32, kind="ExternalInput").ap(),
        "bk": nc.dram_tensor("a0_bk", [128, 8], F32, kind="ExternalInput").ap(),
        "bv": nc.dram_tensor("a0_bv", [1, 1024], F32, kind="ExternalInput").ap(),
    }


def _l0attn_arrays(inputs):
    c = np.ascontiguousarray
    return {
        "a0_wqT": c(inputs["L0_attn_wq"].T),
        "a0_wkT": c(inputs["L0_attn_wk"].T),
        "a0_wvT": c(inputs["L0_attn_wv"].T),
        "a0_bq": c(inputs["L0_attn_bq"].reshape(8, 128).T),
        "a0_bk": c(inputs["L0_attn_bk"].reshape(8, 128).T),
        "a0_bv": c(inputs["L0_attn_bv"].reshape(1, 1024)),
    }


def build_l0attn_program(nqb=None):
    nc = bass.Bass("TRN2", target_bir_lowering=False)
    xin = nc.dram_tensor("xin", [SEQ, D], F32, kind="ExternalInput").ap()
    xout = nc.dram_tensor("xout", [SEQ, D], F32, kind="ExternalOutput").ap()
    W = _l0attn_dram(nc)
    env = {}
    pr = Prog(nc)
    with nc.Block() as block:
        common_init(nc, pr, env)
        build_l0attn(nc, pr, env, xin, xout, W, "a", nqb=nqb)
        pr.run_block(block)
    return nc


def _l1attn_dram(nc):
    return {
        "wqT": nc.dram_tensor("a1_wqT", [1024, 1024], F32, kind="ExternalInput").ap(),
        "wkT": nc.dram_tensor("a1_wkT", [1024, 1024], F32, kind="ExternalInput").ap(),
        "wvT": nc.dram_tensor("a1_wvT", [1024, 1024], F32, kind="ExternalInput").ap(),
        "woLo": nc.dram_tensor("a1_woLo", [64, 16384], F32, kind="ExternalInput").ap(),
    }


def _l1attn_arrays(inputs):
    c = np.ascontiguousarray
    return {
        "a1_wqT": c(inputs["L1_attn_wq"].T),
        "a1_wkT": c(inputs["L1_attn_wk"].T),
        "a1_wvT": c(inputs["L1_attn_wv"].T),
        "a1_woLo": c(np.asarray(inputs["L1_attn_wo"]).T.reshape(16, 64, 1024)
                     .transpose(1, 0, 2).reshape(64, 16384)),
    }


def build_l1attn_program(nheads=None):
    nc = bass.Bass("TRN2", target_bir_lowering=False)
    xin = nc.dram_tensor("xin", [SEQ, D], F32, kind="ExternalInput").ap()
    xout = nc.dram_tensor("xout", [SEQ, D], F32, kind="ExternalOutput").ap()
    W = _l1attn_dram(nc)
    env = {}
    pr = Prog(nc)
    with nc.Block() as block:
        common_init(nc, pr, env)
        build_l1attn(nc, pr, env, xin, xout, W, "h", nheads=nheads)
        pr.run_block(block)
    return nc
